# version 03
# baseline (speedup 1.0000x reference)
import contextlib
import numpy as np
import ml_dtypes
import concourse.bass as bass
import concourse.mybir as mybir
from concourse.bass_utils import run_bass_kernel_spmd

F32 = mybir.dt.float32
BF16 = mybir.dt.bfloat16
AF = mybir.ActivationFunctionType
ALU = mybir.AluOpType
AX = mybir.AxisListType

T = 4096
TH = T // 2
C = 1024
NH = 16
HEAD = 64
L = 64
CMID = 32
SEG = 512
NSEG = T // SEG
NCH = C // 128
DH = 4096
CLAMP = 2.6
SOFT_SHIFT = 16.0
LNEPS = 1e-5
GNEPS = 0.00064
NEG = -30000.0

ENGINES = ("sync", "gpsimd", "vector", "scalar", "tensor")


class Prog:
    def __init__(self):
        self.fns = {k: [] for k in ENGINES}
        self.cnt = {}
        self.sem = {}

    def add_sem(self, name, handle):
        self.sem[name] = handle
        self.cnt[name] = 0

    def emit(self, eng, fn):
        self.fns[eng].append(fn)

    def inc(self, sname, n=1):
        self.cnt[sname] += n
        return self.cnt[sname]

    def now(self, sname):
        return self.cnt[sname]

    def op(self, eng, method, *args, inc=None, incn=1, **kw):
        if eng == "tensor" and method in ("matmul", "transpose") and inc is None:
            inc = "Tn"
        v = None
        if inc is not None:
            v = self.inc(inc, incn)
            sm = self.sem[inc]

            def fn(e, m=method, a=args, k=kw, sm=sm, n=incn):
                getattr(e, m)(*a, **k).then_inc(sm, n)
        else:
            def fn(e, m=method, a=args, k=kw):
                getattr(e, m)(*a, **k)
        self.emit(eng, fn)
        return v

    def wait(self, eng, sname, val=None):
        if val is None:
            val = self.cnt[sname]
        if val <= 0:
            return
        sm = self.sem[sname]
        self.emit(eng, lambda e, s=sm, v=val: e.wait_ge(s, v))

    def dma(self, queue, dst, src, inc=None, **kw):
        if inc is None:
            inc = "dS" if queue == "sync" else "dG"
        return self.op(queue, "dma_start", dst, src, inc=inc, incn=16, **kw)


def ts(i, n=128):
    return slice(i * n, (i + 1) * n)



def _col128(v):
    return np.ascontiguousarray(v.reshape(-1, 128).T)


def prep_host(inputs):
    p = inputs
    sh = {}
    f32 = np.float32
    for nm in ("maa_x", "maa_w", "maa_k", "maa_v", "maa_r", "maa_g",
               "time_decay", "ln_g", "ln_b"):
        sh["c_" + nm] = _col128(np.asarray(p["L0_attn_a_" + nm], f32))
    sh["c_u"] = _col128(np.asarray(p["L0_attn_a_u"], f32).reshape(-1))
    sh["maa_w1"] = np.asarray(p["L0_attn_a_maa_w1"], f32)
    sh["maa_w2"] = np.asarray(p["L0_attn_a_maa_w2"], f32)
    sh["gate_w1"] = np.asarray(p["L0_attn_a_gate_w1"], f32)
    sh["gate_w2"] = np.asarray(p["L0_attn_a_gate_w2"], f32)
    sh["decay_w1"] = np.asarray(p["L0_attn_a_decay_w1"], f32)
    sh["decay_w2"] = np.asarray(p["L0_attn_a_decay_w2"], f32)
    sh["Wr"] = np.asarray(p["L0_attn_a_Wr"], f32)
    sh["Wk"] = np.asarray(p["L0_attn_a_Wk"], f32)
    sh["Wv"] = np.asarray(p["L0_attn_a_Wv"], f32)
    sh["Wo"] = np.asarray(p["L0_attn_a_Wo"], f32)
    sh["Wattn"] = np.asarray(p["L0_attn_b_W_attn"], f32)
    sh["Wproj"] = np.asarray(p["L0_attn_b_W_proj"], f32)
    sh["w1T"] = np.ascontiguousarray(np.asarray(p["L0_ffn_w1_w"], f32).T)
    sh["w2T"] = np.ascontiguousarray(np.asarray(p["L0_ffn_w2_w"], f32).T)
    sh["h1T"] = np.ascontiguousarray(np.asarray(p["L0_ffn_h1_w"], f32).T)
    sh["h2T"] = np.ascontiguousarray(np.asarray(p["L0_ffn_h2_w"], f32).T)
    sh["w3T"] = np.ascontiguousarray(np.asarray(p["L0_ffn_w3_w"], f32).T)
    sh["c_w1s"] = _col128(np.asarray(p["L0_ffn_w1_s"], f32))
    sh["c_w2s"] = _col128(np.asarray(p["L0_ffn_w2_s"], f32))
    sh["c_h1s"] = _col128(np.asarray(p["L0_ffn_h1_s"], f32))
    sh["c_h2s"] = _col128(np.asarray(p["L0_ffn_h2_s"], f32))
    sh["bc_w3s"] = np.ascontiguousarray(
        np.broadcast_to(np.asarray(p["L0_ffn_w3_s"], f32), (128, C)))
    sh["c_battn"] = _col128(np.asarray(p["L0_attn_b_b_attn"], f32))
    sh["bc_bv"] = np.ascontiguousarray(
        np.broadcast_to(np.asarray(p["L0_attn_b_b_attn"], f32)[2 * C:], (128, C)))
    sh["bc_bproj"] = np.ascontiguousarray(
        np.broadcast_to(np.asarray(p["L0_attn_b_b_proj"], f32), (128, C)))
    sh["identity16"] = np.eye(128, dtype=ml_dtypes.bfloat16)
    tri = np.triu(np.ones((L, L), f32), 1)
    dia = np.eye(L, dtype=f32)
    sh["wkv_smask"] = np.ascontiguousarray(np.concatenate([tri, tri], 0))
    sh["wkv_dmask"] = np.ascontiguousarray(np.concatenate([dia, dia], 0))
    go = np.zeros((128, 2), f32)
    go[:HEAD, 0] = 1.0 / HEAD
    go[HEAD:, 1] = 1.0 / HEAD
    sh["gn_ones"] = go
    ind = np.zeros((2, 128), f32)
    ind[0, :HEAD] = 1.0
    ind[1, HEAD:] = 1.0
    sh["gn_ind"] = ind
    sh["ones_col"] = np.ones((128, 1), f32)

    cores = []
    x_all = np.asarray(p["L0_attn_a_x"], f32)
    ktile_tri = np.triu(np.ones((128, 128), f32))
    for core in range(8):
        par = core % 2
        b = core // 2
        d = {"x": np.ascontiguousarray(x_all[b])}
        KB = np.full((128, 8 * 34), -SOFT_SHIFT, f32)
        for g in range(8):
            jmax_glob = 2 * g + 1 + 16 * par
            for i in range(18 + 2 * g):
                if i > jmax_glob:
                    KB[:, g * 34 + i] = NEG - SOFT_SHIFT
        d["kbias"] = np.ascontiguousarray(KB)

        def mk(i_rel):
            m = np.ones((128, 256), f32)
            for sub in range(2):
                kk = i_rel
                qq = sub + 16 * par
                if kk == qq:
                    m[:, sub * 128:(sub + 1) * 128] = ktile_tri
                elif kk > qq:
                    m[:, sub * 128:(sub + 1) * 128] = 0.0
            return m
        d["trimask"] = np.ascontiguousarray(np.concatenate(
            [mk(0), mk(1), mk(16), mk(17)], axis=1))
        d["half0"] = np.full((128, 1), 1.0 - par, f32)
        d["half1"] = np.full((128, 1), float(par), f32)
        cores.append(d)
    return sh, cores



SMALL_LIST = [("c_maa_x", (128, NCH)), ("c_maa_w", (128, NCH)),
              ("c_maa_k", (128, NCH)), ("c_maa_v", (128, NCH)),
              ("c_maa_r", (128, NCH)), ("c_maa_g", (128, NCH)),
              ("c_time_decay", (128, NCH)), ("c_ln_g", (128, NCH)),
              ("c_ln_b", (128, NCH)), ("c_u", (128, NCH)),
              ("c_w1s", (128, 32)), ("c_w2s", (128, 32)),
              ("c_h1s", (128, 32)), ("c_h2s", (128, 32)),
              ("bc_w3s", (128, C)), ("bc_bv", (128, C)),
              ("bc_bproj", (128, C)), ("c_battn", (128, 24)),
              ("wkv_smask", (128, L)), ("wkv_dmask", (128, L)),
              ("gn_ones", (128, 2)), ("gn_ind", (2, 128)),
              ("ones_col", (128, 1)), ("kbias", (128, 8 * 34)),
              ("trimask", (128, 4 * 256)),
              ("half0", (128, 1)), ("half1", (128, 1))]

SMALL_W16 = [("maa_w1", C, 160), ("gate_w1", C, HEAD), ("decay_w1", C, HEAD)]
SMALL_W16_64 = [("gate_w2", HEAD, C), ("decay_w2", HEAD, C)]

BIG_W = [("Wr", C, C), ("Wk", C, C), ("Wv", C, C), ("Wo", C, C),
         ("Wattn", C, 3 * C), ("Wproj", C, C),
         ("w1T", C, DH), ("w2T", C, DH),
         ("h1T", DH, DH), ("h2T", DH, DH), ("w3T", DH, C)]


def build_program(taps=(), phases=("a", "f1", "b", "f2")):
    nc = bass.Bass("TRN2", target_bir_lowering=False)
    es = contextlib.ExitStack()
    tap_set = set(taps)

    di = {}
    for nm, shape in SMALL_LIST:
        di[nm] = nc.dram_tensor(nm, list(shape), F32, kind="ExternalInput")
    for nm, r, c_ in SMALL_W16 + SMALL_W16_64:
        di[nm] = nc.dram_tensor(nm, [r, c_], F32, kind="ExternalInput")
    di["maa_w2"] = nc.dram_tensor("maa_w2", [5, 32, C], F32, kind="ExternalInput")
    for nm, r, c_ in BIG_W:
        di[nm] = nc.dram_tensor(nm, [r, c_], F32, kind="ExternalInput")
    di["x"] = nc.dram_tensor("x", [T, C], F32, kind="ExternalInput")
    d_id16 = nc.dram_tensor("identity16", [128, 128], BF16, kind="ExternalInput")

    out = nc.dram_tensor("out", [TH, C], F32, kind="ExternalOutput")
    d_taps = {}
    tap_shapes = {"x1": (T, C), "x2": (T, C), "x3my": (TH, C),
                  "qf": (C, TH), "kf": (C, T), "vt": (T, NH * 65)}
    tap16 = {"a_r", "a_k", "a_g", "a_yn", "a_ln", "a_v", "kf", "vt", "qf", "yb"}
    for nm in ("a_r", "a_k", "a_g", "a_yn", "a_ln"):
        tap_shapes[nm] = (C, T)
    tap_shapes["a_A"] = (C, T)
    tap_shapes["yb"] = (C, TH)
    tap_shapes["a_y"] = (C, T)
    tap_shapes["a_v"] = (T, C)
    for tname in tap_set:
        d_taps[tname] = nc.dram_tensor("tap_" + tname,
                                       list(tap_shapes[tname]),
                                       BF16 if tname in tap16 else F32,
                                       kind="ExternalOutput")

    d_x1 = nc.dram_tensor("s_x1", [T, C], F32)
    d_x2 = nc.dram_tensor("s_x2", [T, C], F32)
    d_x3my = nc.dram_tensor("s_x3my", [TH, C], F32)
    d_K = nc.dram_tensor("s_K", [C, T], BF16)
    d_V = nc.dram_tensor("s_V", [T, NH * 65], BF16)
    dw = {}
    for nm, r, c_ in BIG_W:
        dw[nm] = nc.dram_tensor("w16_" + nm, [r, c_], BF16)

    P = Prog()
    with es:
        def sb(name, shape, dt=F32):
            return es.enter_context(nc.sbuf_tensor(name, list(shape), dt))

        def psum(name, shape, dt=F32):
            return es.enter_context(nc.psum_tensor(name, list(shape), dt))

        for s in ("dS", "dG", "V", "A", "Tn"):
            P.add_sem(s, es.enter_context(nc.semaphore(s)))

        r_small = {nm: sb("r_" + nm, shape) for nm, shape in SMALL_LIST}
        r_id16 = sb("r_id16", (128, 128), BF16)
        ones64 = sb("ones64", (128, L))
        epscol = sb("epscol", (128, 2))

        pA = psum("pA", (128, 512))
        pB = psum("pB", (128, 512))
        pC = psum("pC", (128, 512))
        pD = psum("pD", (128, 512))
        pE = psum("pE", (128, 512))
        pF = psum("pF", (128, 512))
        pT = psum("pT", (128, 512), BF16)

        xt = [sb(f"xt{i}", (128, C)) for i in range(2)]
        lnscr = sb("lnscr", (128, C))
        st_c0 = sb("st_c0", (128, 32))
        st_c1 = sb("st_c1", (128, 32))
        st_m = sb("st_m", (128, 32))
        st_r = sb("st_r", (128, 32))
        ln_tm = [sb(f"ln_tm{i}", (128, C), BF16) for i in range(2)]
        wbuf = [sb(f"wbuf{i}", (128, 512)) for i in range(2)]
        wb16 = [sb(f"wb16{i}", (128, 512), BF16) for i in range(4)]
        wst = [sb(f"wst{i}", (128, 512), BF16) for i in range(8)]
        es_w16 = contextlib.ExitStack()
        def sbw(name, shape, dt=BF16):
            return es_w16.enter_context(nc.sbuf_tensor(name, list(shape), dt))
        w16 = {}
        for nm, r, c_ in SMALL_W16:
            w16[nm] = sbw("w16_" + nm, (128, (r // 128) * c_))
        for nm, r, c_ in SMALL_W16_64:
            w16[nm] = sbw("w16_" + nm, (r, c_))
        w16["maa_w2"] = sbw("w16_maa_w2", (32, 5 * C))
        r_ublk = sbw("r_ublk", (128, 2 * NCH))


        def load_small_consts():
            for nm, _ in SMALL_LIST:
                P.dma("sync", r_small[nm][:, :], di[nm][:, :])
            P.dma("sync", r_id16[:, :], d_id16[:, :])
            vs = P.now("dS")
            P.wait("vector", "dS", vs)
            P.op("vector", "memset", ones64[:, :], 1.0)
            P.op("vector", "memset", epscol[:, 0:1], LNEPS)
            P.op("vector", "memset", epscol[:, 1:2], GNEPS, inc="V")
            for nm, rows, cols in SMALL_W16:
                for r in range(rows // 128):
                    P.wait("sync", "V")
                    P.dma("sync", xt[0][:, :cols], di[nm][ts(r), :])
                    P.wait("vector", "dS")
                    P.op("vector", "tensor_copy",
                         w16[nm][:, r * cols:(r + 1) * cols], xt[0][:, :cols],
                         inc="V")
            for nm, rows, cols in SMALL_W16_64:
                P.wait("sync", "V")
                P.dma("sync", xt[0][0:rows, :cols], di[nm][:, :])
                P.wait("vector", "dS")
                P.op("vector", "tensor_copy", w16[nm][0:rows, :],
                     xt[0][0:rows, :cols], inc="V")
            for f in range(5):
                P.wait("sync", "V")
                P.dma("sync", xt[0][0:32, :C], di["maa_w2"][f, :, :])
                P.wait("vector", "dS")
                P.op("vector", "tensor_copy",
                     w16["maa_w2"][0:32, f * C:(f + 1) * C], xt[0][0:32, :C],
                     inc="V")
            P.wait("vector", "dS", vs)
            P.op("vector", "memset", r_ublk[:, :], 0.0, inc="V")
            P.wait("vector", "V")
            for pc in range(NCH):
                P.op("vector", "tensor_copy", r_ublk[0:64, 2 * pc:2 * pc + 1],
                     r_small["c_u"][0:64, pc:pc + 1])
                P.op("vector", "tensor_copy",
                     r_ublk[64:128, 2 * pc + 1:2 * pc + 2],
                     r_small["c_u"][64:128, pc:pc + 1],
                     inc=("V" if pc == NCH - 1 else None))

        def prep_weights16(names):
            k = 0
            for nm, rows, cols in BIG_W:
                if nm not in names:
                    continue
                for r in range(rows // 128):
                    for c0 in range(0, cols, 512):
                        w = min(512, cols - c0)
                        buf = k % 2
                        cb = k % 4
                        eng, s = (("vector", "V") if k % 2 == 0
                                  else ("scalar", "A"))
                        P.wait("sync", s)
                        P.dma("sync", wbuf[buf][:, :w], di[nm][ts(r), c0:c0 + w])
                        P.wait(eng, "dS")
                        P.wait(eng, "dG")
                        if eng == "vector":
                            P.op(eng, "tensor_copy", wb16[cb][:, :w],
                                 wbuf[buf][:, :w], inc=s)
                        else:
                            P.op(eng, "activation", wb16[cb][:, :w],
                                 wbuf[buf][:, :w], AF.Identity, inc=s)
                        P.wait("gpsimd", s)
                        P.dma("gpsimd", dw[nm][ts(r), c0:c0 + w], wb16[cb][:, :w])
                        k += 1
            P.wait("sync", "dG")

        def emit_ln_fm(src_dram, n_tiles, fm_bufs, fm_col0, tile0=0):
            for t in range(n_tiles):
                buf = t % 2
                P.wait("sync", "A")
                P.wait("sync", "V")
                P.wait("sync", "dG")
                P.dma("sync", xt[buf][:, :], src_dram[ts(tile0 + t), :])
                P.wait("scalar", "dS")
                P.op("scalar", "activation", lnscr[:, :], xt[buf][:, :],
                     AF.Square, accum_out=st_c0[:, t:t + 1], inc="A")
                P.wait("vector", "dS")
                P.op("vector", "reduce_sum", st_c1[:, t:t + 1], xt[buf][:, :],
                     AX.X, inc="V")
            nt = n_tiles
            P.wait("vector", "A")
            P.wait("vector", "V")
            P.op("vector", "tensor_scalar_mul", st_m[:, :nt], st_c1[:, :nt],
                 1.0 / C, inc="V")
            P.wait("vector", "V")
            P.op("vector", "tensor_tensor", st_r[:, :nt], st_m[:, :nt],
                 st_m[:, :nt], ALU.mult, inc="V")
            P.wait("vector", "V")
            P.op("vector", "scalar_tensor_tensor", st_r[:, :nt], st_c0[:, :nt],
                 1.0 / C, st_r[:, :nt], ALU.mult, ALU.subtract, inc="V")
            P.wait("scalar", "V")
            P.op("scalar", "activation", st_r[:, :nt], st_r[:, :nt], AF.Ln,
                 bias=epscol[:, 0:1], inc="A")
            P.wait("scalar", "A")
            P.op("scalar", "activation", st_r[:, :nt], st_r[:, :nt], AF.Exp,
                 scale=-0.5, inc="A")
            for t in range(n_tiles):
                buf = t % 2
                P.wait("sync", "V")
                P.wait("sync", "dG")
                P.dma("sync", xt[buf][:, :], src_dram[ts(tile0 + t), :])
                P.wait("vector", "dS")
                P.wait("vector", "A")
                P.op("vector", "tensor_scalar", ln_tm[buf][:, :], xt[buf][:, :],
                     st_m[:, t:t + 1], st_r[:, t:t + 1], ALU.subtract, ALU.mult,
                     inc="V")
                P.wait("tensor", "V")
                for ch in range(NCH):
                    P.op("tensor", "transpose", pT[0:128, 0:128],
                         ln_tm[buf][:, ts(ch)], r_id16[:, :], inc="Tn")
                    P.wait("vector", "Tn")
                    P.op("vector", "tensor_copy",
                         fm_bufs[ch][:, fm_col0 + 128 * t: fm_col0 + 128 * (t + 1)],
                         pT[0:128, 0:128], inc="V")
                    P.wait("tensor", "V")

        def emit_attn_a(es_p):
            def sb(name, shape, dt=F32):
                return es_p.enter_context(nc.sbuf_tensor(name, list(shape), dt))
            fm = [sb(f"a_fm{c_}", (128, SEG), BF16) for c_ in range(NCH)]
            xx = [sb(f"a_xx{c_}", (128, SEG), BF16) for c_ in range(NCH)]
            xF = [sb(f"a_xF{c_}", (128, SEG), BF16) for c_ in range(NCH)]
            rS = [sb(f"a_r{c_}", (128, SEG), BF16) for c_ in range(NCH)]
            kS = [sb(f"a_k{c_}", (128, SEG), BF16) for c_ in range(NCH)]
            gS = [sb(f"a_g{c_}", (128, SEG), BF16) for c_ in range(NCH)]
            ukS = [sb(f"a_uk{c_}", (128, SEG), BF16) for c_ in range(NCH)]
            rhS = [sb(f"a_rh{c_}", (128, SEG), BF16) for c_ in range(NCH)]
            aScr = sb("a_aScr", (128, SEG))
            AP_ = [sb(f"a_AP{c_}", (128, SEG)) for c_ in range(NCH)]
            vT = [sb(f"a_vT{t}", (128, C), BF16) for t in range(SEG // 128)]
            yS = [sb(f"a_y{c_}", (128, SEG)) for c_ in range(NCH)]
            prevc = sb("a_prevc", (128, NCH), BF16)
            lora = sb("a_lora", (32, 5 * SEG), BF16)
            tanh64 = sb("a_tanh64", (64, SEG), BF16)
            kblk = [sb(f"a_kblk{i}", (128, 128), BF16) for i in range(2)]
            ukblk = [sb(f"a_ukblk{i}", (128, 128), BF16) for i in range(2)]
            khblk = [sb(f"a_khblk{i}", (128, 128), BF16) for i in range(2)]
            khT = [sb(f"a_khT{i}", (128, 128), BF16) for i in range(2)]
            vstk = [sb(f"a_vstk{i}", (128, 128), BF16) for i in range(2)]
            stau = sb("a_stau", (128, 64), BF16)
            stu_d = sb("a_stud", (128, 64))
            sta_f = sb("a_staf", (128, 64))
            S = [sb(f"a_S{pc}", (128, 128)) for pc in range(NCH)]
            S16 = sb("a_S16", (128, 128), BF16)
            ecol = sb("a_ecol", (128, 8))
            e2t = sb("a_e2t", (128, L))
            e1t = sb("a_e1t", (128, L))
            rt_t = sb("a_rtt", (128, L), BF16)
            gnst = sb("a_gnst", (2, SEG))
            gnsq = sb("a_gnsq", (2, SEG))
            gnsc = sb("a_gnsc", (2, SEG))
            ysq = sb("a_ysq", (128, SEG))
            yg = sb("a_yg", (128, SEG), BF16)

            for pc in range(NCH):
                P.op("vector", "memset", S[pc][:, :], 0.0)
            for i in range(2):
                P.op("vector", "memset", kblk[i][:, :], 0.0)
                P.op("vector", "memset", ukblk[i][:, :], 0.0)
                P.op("vector", "memset", khblk[i][:, :], 0.0)
                P.op("vector", "memset", vstk[i][:, :], 0.0)
            P.op("vector", "memset", prevc[:, :], 0.0, inc="V")

            def tapdump(name, bufs, seg):
                if name not in tap_set:
                    return
                P.wait("gpsimd", "V")
                P.wait("gpsimd", "A")
                for ch in range(NCH):
                    P.dma("gpsimd", d_taps[name][ts(ch), seg * SEG:(seg + 1) * SEG],
                          bufs[ch][:, 0:SEG])

            def proj_fm(dst_bufs, src_bufs, wname):
                for oc in range(NCH):
                    psm = pA if oc % 2 == 0 else pB
                    for kc in range(NCH):
                        P.wait("sync", "Tn")
                        P.dma("sync", wst[kc][:, 0:128],
                              dw[wname][ts(kc), ts(oc)])
                        P.wait("tensor", "dS")
                        if kc == 0:
                            P.wait("tensor", "V")
                        P.op("tensor", "matmul", psm[0:128, 0:SEG],
                             wst[kc][:, 0:128], src_bufs[kc][:, :],
                             start=(kc == 0), stop=(kc == NCH - 1),
                             inc=("Tn" if kc == NCH - 1 else None))
                    P.wait("vector", "Tn")
                    P.op("vector", "tensor_copy", dst_bufs[oc][:, :],
                         psm[0:128, 0:SEG], inc="V")

            for seg in range(NSEG):
                t0 = seg * (SEG // 128)
                emit_ln_fm(di["x"], SEG // 128, fm, 0, tile0=t0)
                P.wait("vector", "Tn")
                P.wait("vector", "V")
                for ch in range(NCH):
                    P.op("vector", "tensor_sub", xx[ch][:, 1:SEG],
                         fm[ch][:, 0:SEG - 1], fm[ch][:, 1:SEG])
                    P.op("vector", "tensor_sub", xx[ch][:, 0:1],
                         prevc[:, ch:ch + 1], fm[ch][:, 0:1],
                         inc=("V" if ch == NCH - 1 else None))
                P.wait("vector", "V")
                for ch in range(NCH):
                    P.op("vector", "tensor_copy", prevc[:, ch:ch + 1],
                         fm[ch][:, SEG - 1:SEG],
                         inc=("V" if ch == NCH - 1 else None))
                P.wait("vector", "V")
                for ch in range(NCH):
                    P.op("vector", "scalar_tensor_tensor", xF[ch][:, :],
                         xx[ch][:, :], r_small["c_maa_x"][:, ch:ch + 1],
                         fm[ch][:, :], ALU.mult, ALU.add,
                         inc=("V" if ch == NCH - 1 else None))
                P.wait("tensor", "V")
                for part, (r0, rn) in enumerate(((0, 64), (64, 64), (128, 32))):
                    psm = (pA, pB, pC)[part]
                    for kc in range(NCH):
                        P.op("tensor", "matmul", psm[0:rn, 0:SEG],
                             w16["maa_w1"][:, kc * 160 + r0: kc * 160 + r0 + rn],
                             xF[kc][:, :], start=(kc == 0), stop=(kc == NCH - 1),
                             inc=("Tn" if kc == NCH - 1 else None))
                P.wait("scalar", "Tn")
                for f in range(5):
                    src = (pA, pA, pB, pB, pC)[f]
                    r0 = (0, 32, 0, 32, 0)[f]
                    P.op("scalar", "activation", lora[0:32, f * SEG:(f + 1) * SEG],
                         src[r0:r0 + 32, 0:SEG], AF.Tanh,
                         inc=("A" if f == 4 else None))
                for f, nm in enumerate(("w", "k", "v", "r", "g")):
                    P.wait("tensor", "A")
                    P.wait("vector", "Tn")
                    for ch in range(NCH):
                        psm = pD if ch % 2 == 0 else pE
                        P.wait("tensor", "V")
                        P.op("tensor", "matmul", psm[0:128, 0:SEG],
                             w16["maa_w2"][0:32, f * C + ch * 128: f * C + (ch + 1) * 128],
                             lora[0:32, f * SEG:(f + 1) * SEG],
                             start=True, stop=True, inc="Tn")
                        P.wait("vector", "Tn")
                        P.op("vector", "tensor_scalar_add", ysq[:, 0:SEG],
                             psm[0:128, 0:SEG],
                             r_small["c_maa_" + nm][:, ch:ch + 1], inc="V")
                        P.wait("vector", "V")
                        P.op("vector", "tensor_mul", ysq[:, 0:SEG],
                             ysq[:, 0:SEG], xx[ch][:, :], inc="V")
                        P.wait("vector", "V")
                        P.op("vector", "tensor_add", xF[ch][:, :],
                             ysq[:, 0:SEG], fm[ch][:, :], inc="V")
                    if nm == "r":
                        proj_fm(rS, xF, "Wr")
                        tapdump("a_r", rS, seg)
                    elif nm == "k":
                        proj_fm(kS, xF, "Wk")
                        tapdump("a_k", kS, seg)
                    elif nm == "v":
                        for tt in range(SEG // 128):
                            for oc2 in range(2):
                                for kc in range(NCH):
                                    P.wait("sync", "Tn")
                                    P.dma("sync", wst[kc][:, 0:512],
                                          dw["Wv"][ts(kc),
                                                   oc2 * 512:(oc2 + 1) * 512])
                                    P.wait("tensor", "dS")
                                    if kc == 0:
                                        P.wait("tensor", "V")
                                    P.op("tensor", "matmul", pC[0:128, 0:512],
                                         xF[kc][:, ts(tt)], wst[kc][:, 0:512],
                                         start=(kc == 0), stop=(kc == NCH - 1),
                                         inc=("Tn" if kc == NCH - 1 else None))
                                P.wait("vector", "Tn")
                                P.op("vector", "tensor_copy",
                                     vT[tt][:, oc2 * 512:(oc2 + 1) * 512],
                                     pC[0:128, 0:512], inc="V")
                    elif nm in ("g", "w"):
                        wn1 = "gate_w1" if nm == "g" else "decay_w1"
                        wn2 = "gate_w2" if nm == "g" else "decay_w2"
                        for kc in range(NCH):
                            if kc == 0:
                                P.wait("tensor", "V")
                            P.op("tensor", "matmul", pC[0:64, 0:SEG],
                                 w16[wn1][:, kc * HEAD:(kc + 1) * HEAD],
                                 xF[kc][:, :], start=(kc == 0),
                                 stop=(kc == NCH - 1),
                                 inc=("Tn" if kc == NCH - 1 else None))
                        P.wait("scalar", "Tn")
                        P.op("scalar", "activation", tanh64[0:64, 0:SEG],
                             pC[0:64, 0:SEG], AF.Tanh, inc="A")
                        P.wait("tensor", "A")
                        for ch in range(NCH):
                            P.op("tensor", "matmul", pD[0:128, 0:SEG],
                                 w16[wn2][0:64, ch * 128:(ch + 1) * 128],
                                 tanh64[0:64, 0:SEG], start=True, stop=True,
                                 inc="Tn")
                            if nm == "g":
                                P.wait("vector", "Tn")
                                P.op("vector", "tensor_copy", gS[ch][:, :],
                                     pD[0:128, 0:SEG], inc="V")
                                P.wait("tensor", "V")
                            else:
                                P.wait("scalar", "Tn")
                                P.wait("scalar", "V")
                                P.op("scalar", "activation", aScr[:, :],
                                     pD[0:128, 0:SEG], AF.Exp,
                                     bias=r_small["c_time_decay"][:, ch:ch + 1],
                                     inc="A")
                                P.wait("vector", "A")
                                P.op("vector", "tensor_scalar_min",
                                     aScr[:, :], aScr[:, :], CLAMP, inc="V")
                                P.wait("vector", "V")
                                for cki in range(SEG // L):
                                    P.op("vector", "tensor_tensor_scan",
                                         AP_[ch][:, cki * L:(cki + 1) * L],
                                         ones64[:, :],
                                         aScr[:, cki * L:(cki + 1) * L], 0.0,
                                         ALU.mult, ALU.add,
                                         inc=("V" if cki == SEG // L - 1
                                              else None))
                                P.wait("tensor", "V")
                tapdump("a_g", gS, seg)
                tapdump("a_A", AP_, seg)
                tapdump("a_ln", fm, seg)
                if "a_v" in tap_set:
                    P.wait("gpsimd", "V")
                    for tt in range(SEG // 128):
                        P.dma("gpsimd",
                              d_taps["a_v"][ts(seg * (SEG // 128) + tt), :],
                              vT[tt][:, :])
                P.wait("vector", "V")
                for ch in range(NCH):
                    P.op("vector", "tensor_scalar_mul", ukS[ch][:, :],
                         kS[ch][:, :], r_small["c_u"][:, ch:ch + 1],
                         inc=("V" if ch == NCH - 1 else None))
                P.wait("scalar", "V")
                for ch in range(NCH):
                    for cki in range(SEG // L):
                        c0 = cki * L
                        P.op("vector", "memset", e1t[:, 0:1], 1.0, inc="V")
                        P.wait("scalar", "V")
                        P.op("scalar", "activation", e1t[:, 1:L],
                             AP_[ch][:, c0:c0 + L - 1], AF.Exp, scale=-1.0,
                             inc="A")
                        P.wait("vector", "A")
                        P.op("vector", "tensor_mul", rhS[ch][:, c0:c0 + L],
                             rS[ch][:, c0:c0 + L], e1t[:, 0:L], inc="V")
                        P.wait("scalar", "V")
                for pc in range(NCH):
                    P.wait("vector", "Tn")
                    P.op("vector", "tensor_copy", S16[:, :], S[pc][:, :],
                         inc="V")
                    for cki in range(SEG // L):
                        c0 = cki * L
                        ib = cki % 2
                        P.wait("vector", "A")
                        P.op("vector", "tensor_scalar_mul", ecol[:, 0:1],
                             AP_[pc][:, c0 + CMID - 1:c0 + CMID], -1.0, inc="V")
                        P.wait("vector", "V")
                        P.op("vector", "tensor_scalar_mul", ecol[:, 1:2],
                             ecol[:, 0:1], -1.0, inc="V")
                        P.wait("scalar", "V")
                        P.op("scalar", "activation", e2t[:, 0:L],
                             AP_[pc][:, c0:c0 + L], AF.Exp, bias=ecol[:, 0:1],
                             inc="A")
                        P.op("scalar", "activation", e1t[:, 1:L],
                             AP_[pc][:, c0:c0 + L - 1], AF.Exp, scale=-1.0,
                             bias=ecol[:, 1:2], inc="A")
                        P.wait("scalar", "A")
                        P.op("scalar", "activation", e1t[:, 0:1], ecol[:, 1:2],
                             AF.Exp, inc="A")
                        P.wait("vector", "A")
                        P.op("vector", "tensor_sub", ecol[:, 2:3], ecol[:, 1:2],
                             AP_[pc][:, c0 + L - 1:c0 + L], inc="V")
                        P.wait("scalar", "V")
                        P.op("scalar", "activation", ecol[:, 3:4], ecol[:, 2:3],
                             AF.Exp, inc="A")
                        P.op("scalar", "activation", ecol[:, 4:5],
                             AP_[pc][:, c0 + L - 1:c0 + L], AF.Exp, scale=-1.0,
                             inc="A")
                        P.wait("vector", "A")
                        P.wait("vector", "Tn")
                        P.op("vector", "tensor_mul", kblk[ib][0:64, 0:64],
                             kS[pc][0:64, c0:c0 + L], e2t[0:64, 0:L])
                        P.op("vector", "tensor_mul", kblk[ib][64:128, 64:128],
                             kS[pc][64:128, c0:c0 + L], e2t[64:128, 0:L],
                             inc="V")
                        P.wait("vector", "V")
                        P.op("vector", "tensor_scalar_mul", khblk[ib][:, :],
                             kblk[ib][:, :], ecol[:, 3:4], inc="V")
                        P.op("vector", "tensor_copy", ukblk[ib][0:64, 0:64],
                             ukS[pc][0:64, c0:c0 + L])
                        P.op("vector", "tensor_copy", ukblk[ib][64:128, 64:128],
                             ukS[pc][64:128, c0:c0 + L], inc="V")
                        tt, row0 = divmod(c0, 128)
                        P.op("vector", "tensor_copy", vstk[ib][0:64, 0:64],
                             vT[tt][row0:row0 + 64, pc * 128:pc * 128 + 64])
                        P.op("vector", "tensor_copy", vstk[ib][64:128, 64:128],
                             vT[tt][row0:row0 + 64, pc * 128 + 64:(pc + 1) * 128],
                             inc="V")
                        P.op("vector", "tensor_scalar_mul", rt_t[:, 0:L],
                             rhS[pc][:, c0:c0 + L], e1t[:, 0:1], inc="V")
                        P.wait("tensor", "V")
                        P.op("tensor", "matmul", pC[0:128, 0:64], kblk[ib][:, :],
                             rt_t[:, 0:L], start=True, stop=True, inc="Tn")
                        P.op("tensor", "matmul", pC[0:128, 64:128],
                             ukblk[ib][:, :], rS[pc][:, c0:c0 + L],
                             start=True, stop=True, inc="Tn")
                        P.wait("vector", "Tn")
                        P.op("vector", "tensor_mul", stu_d[:, 0:64],
                             pC[0:128, 64:128], r_small["wkv_dmask"][:, :])
                        P.op("vector", "tensor_mul", sta_f[:, 0:64],
                             pC[0:128, 0:64], r_small["wkv_smask"][:, :],
                             inc="V")
                        P.wait("vector", "V")
                        P.op("vector", "tensor_add", stau[0:128, 0:64],
                             sta_f[:, 0:64], stu_d[:, 0:64], inc="V")
                        P.wait("tensor", "V")
                        P.op("tensor", "matmul", pD[0:128, 0:64], vstk[ib][:, :],
                             stau[0:128, 0:64], start=True, stop=False)
                        P.op("tensor", "matmul", pD[0:128, 0:64], S16[:, :],
                             rhS[pc][:, c0:c0 + L], start=False, stop=True,
                             inc="Tn")
                        P.op("tensor", "transpose", pT[0:128, 128:256],
                             khblk[ib][:, :], r_id16[:, :], inc="Tn")
                        P.wait("vector", "Tn")
                        P.op("vector", "tensor_copy", yS[pc][:, c0:c0 + L],
                             pD[0:128, 0:64])
                        P.op("vector", "tensor_copy", khT[ib][:, :],
                             pT[0:128, 128:256], inc="V")
                        P.wait("tensor", "V")
                        P.op("tensor", "matmul", pE[0:128, 0:128], khT[ib][:, :],
                             vstk[ib][:, :], start=True, stop=True, inc="Tn")
                        P.wait("vector", "Tn")
                        P.op("vector", "scalar_tensor_tensor", S[pc][:, :],
                             S[pc][:, :], ecol[:, 4:5], pE[0:128, 0:128],
                             ALU.mult, ALU.add, inc="V")
                        P.wait("vector", "V")
                        P.op("vector", "tensor_copy", S16[:, :], S[pc][:, :],
                             inc="V")
                tapdump("a_y", yS, seg)
                for pc in range(NCH):
                    P.wait("scalar", "V")
                    P.op("scalar", "activation", ysq[:, 0:SEG], yS[pc][:, :],
                         AF.Square, inc="A")
                    P.wait("tensor", "A")
                    P.wait("tensor", "V")
                    P.op("tensor", "matmul", pC[0:2, 0:SEG],
                         r_small["gn_ones"][:, 0:2], yS[pc][:, :],
                         start=True, stop=True, inc="Tn")
                    P.op("tensor", "matmul", pD[0:2, 0:SEG],
                         r_small["gn_ones"][:, 0:2], ysq[:, 0:SEG],
                         start=True, stop=True, inc="Tn")
                    P.wait("vector", "Tn")
                    P.op("vector", "tensor_copy", gnst[0:2, :], pC[0:2, 0:SEG])
                    P.op("vector", "tensor_copy", gnsq[0:2, :], pD[0:2, 0:SEG],
                         inc="V")
                    P.wait("vector", "V")
                    P.op("vector", "tensor_tensor", gnsc[0:2, :], gnst[0:2, :],
                         gnst[0:2, :], ALU.mult, inc="V")
                    P.wait("vector", "V")
                    P.op("vector", "tensor_sub", gnsq[0:2, :], gnsq[0:2, :],
                         gnsc[0:2, :], inc="V")
                    P.wait("scalar", "V")
                    P.op("scalar", "activation", gnsq[0:2, :], gnsq[0:2, :],
                         AF.Ln, bias=epscol[0:2, 1:2], inc="A")
                    P.wait("scalar", "A")
                    P.op("scalar", "activation", gnsq[0:2, :], gnsq[0:2, :],
                         AF.Exp, scale=-0.5, inc="A")
                    P.wait("tensor", "A")
                    P.op("tensor", "matmul", pC[0:128, 0:SEG],
                         r_small["gn_ind"][0:2, :], gnst[0:2, :],
                         start=True, stop=True, inc="Tn")
                    P.op("tensor", "matmul", pD[0:128, 0:SEG],
                         r_small["gn_ind"][0:2, :], gnsq[0:2, :],
                         start=True, stop=True, inc="Tn")
                    P.wait("vector", "Tn")
                    P.op("vector", "tensor_sub", ysq[:, 0:SEG], yS[pc][:, :],
                         pC[0:128, 0:SEG], inc="V")
                    P.wait("vector", "V")
                    P.op("vector", "tensor_mul", ysq[:, 0:SEG], ysq[:, 0:SEG],
                         pD[0:128, 0:SEG], inc="V")
                    P.wait("vector", "V")
                    P.op("vector", "tensor_scalar", ysq[:, 0:SEG],
                         ysq[:, 0:SEG], r_small["c_ln_g"][:, pc:pc + 1],
                         r_small["c_ln_b"][:, pc:pc + 1], ALU.mult, ALU.add,
                         inc="V")
                    P.wait("vector", "V")
                    P.op("vector", "tensor_mul", xF[pc][:, :], ysq[:, 0:SEG],
                         gS[pc][:, :], inc="V")
                tapdump("a_yn", xF, seg)
                for tt in range(SEG // 128):
                    P.wait("sync", "V")
                    P.wait("sync", "dG")
                    P.dma("sync", xt[tt % 2][:, :], di["x"][ts(t0 + tt), :])
                    for oc2 in range(2):
                        for kc in range(NCH):
                            P.wait("sync", "Tn")
                            P.dma("sync", wst[kc][:, 0:512],
                                  dw["Wo"][ts(kc), oc2 * 512:(oc2 + 1) * 512])
                            P.wait("tensor", "dS")
                            if kc == 0:
                                P.wait("tensor", "V")
                            P.op("tensor", "matmul", pE[0:128, 0:512],
                                 xF[kc][:, ts(tt)], wst[kc][:, 0:512],
                                 start=(kc == 0), stop=(kc == NCH - 1),
                                 inc=("Tn" if kc == NCH - 1 else None))
                        P.wait("vector", "Tn")
                        P.wait("vector", "dS")
                        P.op("vector", "tensor_add",
                             xt[tt % 2][:, oc2 * 512:(oc2 + 1) * 512],
                             xt[tt % 2][:, oc2 * 512:(oc2 + 1) * 512],
                             pE[0:128, 0:512], inc="V")
                    P.wait("gpsimd", "V")
                    P.dma("gpsimd", d_x1[ts(t0 + tt), :], xt[tt % 2][:, :])
            P.wait("sync", "dG")

        def emit_ffn(es_p, src_dram, dst_dram, n_tok, tag=""):
            def sb(name, shape, dt=F32):
                return es_p.enter_context(nc.sbuf_tensor(tag + name, list(shape), dt))
            QT = 1024
            lnq = [sb(f"f_ln{c_}", (128, QT), BF16) for c_ in range(NCH)]
            tbuf = [sb(f"f_t{c_}", (128, QT), BF16) for c_ in range(32)]
            hbuf = [sb(f"f_h{c_}", (128, QT), BF16) for c_ in range(32)]
            t1s = sb("f_t1s", (128, QT))
            t2s = sb("f_t2s", (128, QT))
            for q in range(n_tok // QT):
                tile0 = q * (QT // 128)
                emit_ln_fm(src_dram, QT // 128, lnq, 0, tile0=tile0)
                for oc in range(32):
                    for wi, wnm in enumerate(("w1T", "w2T")):
                        for kc in range(NCH):
                            P.wait("sync", "Tn")
                            P.dma("sync", wst[kc][:, 0:128],
                                  dw[wnm][ts(kc), ts(oc)])
                        P.wait("tensor", "dS")
                        if wi == 0:
                            P.wait("tensor", "V")
                            P.wait("tensor", "A")
                        for g in range(2):
                            psm = (pA, pB, pC, pD)[2 * wi + g]
                            for kc in range(NCH):
                                P.op("tensor", "matmul", psm[0:128, 0:512],
                                     wst[kc][:, 0:128],
                                     lnq[kc][:, g * 512:(g + 1) * 512],
                                     start=(kc == 0), stop=(kc == NCH - 1))
                    P.wait("scalar", "Tn")
                    P.wait("scalar", "V")
                    for g in range(2):
                        P.op("scalar", "activation",
                             t1s[:, g * 512:(g + 1) * 512],
                             (pA, pB)[g][0:128, 0:512], AF.Silu,
                             scale=r_small["c_w1s"][:, oc:oc + 1], inc="A")
                    P.wait("vector", "Tn")
                    for g in range(2):
                        P.op("vector", "tensor_scalar_mul",
                             t2s[:, g * 512:(g + 1) * 512],
                             (pC, pD)[g][0:128, 0:512],
                             r_small["c_w2s"][:, oc:oc + 1], inc="V")
                    P.wait("vector", "A")
                    P.wait("vector", "V")
                    P.op("vector", "tensor_mul", tbuf[oc][:, :], t1s[:, :],
                         t2s[:, :], inc="V")
                for stage, (wnm, snm, sbr, dbr) in enumerate(
                        (("h1T", "c_h1s", tbuf, hbuf),
                         ("h2T", "c_h2s", hbuf, tbuf))):
                    for oc in range(32):
                        for kc in range(32):
                            P.wait("sync", "Tn")
                            P.dma("sync", wst[kc % 8][:, 0:128],
                                  dw[wnm][ts(kc), ts(oc)])
                            P.wait("tensor", "dS")
                            if oc == 0 and kc == 0:
                                P.wait("tensor", "V")
                            for g in range(2):
                                P.op("tensor", "matmul",
                                     (pA, pB)[g][0:128, 0:512],
                                     wst[kc % 8][:, 0:128],
                                     sbr[kc][:, g * 512:(g + 1) * 512],
                                     start=(kc == 0), stop=(kc == 31),
                                     skip_group_check=True)
                        P.wait("scalar", "Tn")
                        for g in range(2):
                            P.op("scalar", "activation",
                                 dbr[oc][:, g * 512:(g + 1) * 512],
                                 (pA, pB)[g][0:128, 0:512], AF.Gelu,
                                 scale=r_small[snm][:, oc:oc + 1], inc="A")
                        P.wait("tensor", "A")
                for tt in range(QT // 128):
                    P.wait("sync", "V")
                    P.wait("sync", "dG")
                    P.dma("sync", xt[tt % 2][:, :], src_dram[ts(tile0 + tt), :])
                    for og in range(2):
                        for kc in range(32):
                            P.wait("sync", "Tn")
                            P.dma("sync", wst[kc % 8][:, 0:512],
                                  dw["w3T"][ts(kc), og * 512:(og + 1) * 512])
                            P.wait("tensor", "dS")
                            if kc == 0:
                                P.wait("tensor", "V")
                                P.wait("tensor", "A")
                            P.op("tensor", "matmul", pE[0:128, 0:512],
                                 tbuf[kc][:, ts(tt)], wst[kc % 8][:, 0:512],
                                 start=(kc == 0), stop=(kc == 31))
                        P.wait("vector", "Tn")
                        P.op("vector", "tensor_tensor", t1s[:, 0:512],
                             pE[0:128, 0:512],
                             r_small["bc_w3s"][:, og * 512:(og + 1) * 512],
                             ALU.mult, inc="V")
                        P.wait("vector", "V")
                        P.wait("vector", "dS")
                        P.op("vector", "tensor_add",
                             xt[tt % 2][:, og * 512:(og + 1) * 512],
                             xt[tt % 2][:, og * 512:(og + 1) * 512],
                             t1s[:, 0:512], inc="V")
                    P.wait("gpsimd", "V")
                    P.dma("gpsimd", dst_dram[ts(tile0 + tt), :],
                          xt[tt % 2][:, :])
            P.wait("sync", "dG")

        def emit_attn_b(es_p):
            def sb(name, shape, dt=F32):
                return es_p.enter_context(nc.sbuf_tensor(name, list(shape), dt))
            myQ = [sb(f"b_q{c_}", (128, TH), BF16) for c_ in range(NCH)]
            Yfm = [sb(f"b_y{c_}", (128, TH), BF16) for c_ in range(NCH)]
            lnq = [sb(f"b_ln{c_}", (128, 512), BF16) for c_ in range(NCH)]
            scrq = sb("b_scrq", (128, 512))
            kb = [[sb(f"b_kb{i}_{c_}", (128, 128), BF16) for c_ in range(NCH)]
                  for i in range(2)]
            vsb = [sb(f"b_vsb{i}", (128, 520), BF16) for i in range(2)]
            pbuf = [sb(f"b_p{i}", (128, 256), BF16) for i in range(4)]
            yscr = sb("b_yscr", (128, 256))
            lrec = sb("b_lrec", (128, 256))
            ones1 = sb("b_ones1", (1, 64))
            x2my = sb("b_x2my", (128, C))
            x2sc = sb("b_x2sc", (128, C))
            P.op("vector", "memset", ones1[0:1, :], 1.0, inc="V")
            vwrite = sb("b_vwrite", (128, NH * 65), BF16)
            P.op("vector", "memset", vwrite[:, :], 1.0, inc="V")
            for g5 in range(T // 512):
                tile0 = g5 * 4
                emit_ln_fm(d_x2, 4, lnq, 0, tile0=tile0)
                for oc in range(16):
                    psm = pA if oc % 2 == 0 else pB
                    for kc in range(NCH):
                        P.wait("sync", "Tn")
                        P.dma("sync", wst[kc][:, 0:128],
                              dw["Wattn"][ts(kc), ts(oc)])
                        P.wait("tensor", "dS")
                        if kc == 0:
                            P.wait("tensor", "V")
                        P.op("tensor", "matmul", psm[0:128, 0:512],
                             wst[kc][:, 0:128], lnq[kc][:, :],
                             start=(kc == 0), stop=(kc == NCH - 1))
                    P.wait("vector", "Tn")
                    if oc < 8:
                        dest0 = (g5 % 4) * 512
                        half = r_small["half0"] if g5 < 4 else r_small["half1"]
                        if g5 < 4:
                            P.op("vector", "tensor_scalar", myQ[oc][:, dest0:dest0 + 512],
                                 psm[0:128, 0:512], r_small["c_battn"][:, oc:oc + 1],
                                 half[:, 0:1], ALU.add, ALU.mult, inc="V")
                        else:
                            P.op("vector", "tensor_scalar", scrq[:, 0:512],
                                 psm[0:128, 0:512], r_small["c_battn"][:, oc:oc + 1],
                                 half[:, 0:1], ALU.add, ALU.mult, inc="V")
                            P.wait("vector", "V")
                            P.op("vector", "tensor_add", myQ[oc][:, dest0:dest0 + 512],
                                 myQ[oc][:, dest0:dest0 + 512], scrq[:, 0:512],
                                 inc="V")
                    else:
                        kc8 = oc - 8
                        P.wait("vector", "dG")
                        P.op("vector", "tensor_scalar_add", scrq[:, 0:512],
                             psm[0:128, 0:512],
                             r_small["c_battn"][:, 8 + kc8:9 + kc8], inc="V")
                        P.wait("gpsimd", "V")
                        P.dma("gpsimd", d_K[ts(kc8), g5 * 512:(g5 + 1) * 512],
                              scrq[:, 0:512])
                for tt in range(4):
                    for og in range(2):
                        for kc in range(NCH):
                            P.wait("sync", "Tn")
                            P.dma("sync", wst[kc][:, 0:512],
                                  dw["Wattn"][ts(kc),
                                              2 * C + og * 512: 2 * C + (og + 1) * 512])
                            P.wait("tensor", "dS")
                            if kc == 0:
                                P.wait("tensor", "V")
                            P.op("tensor", "matmul", pC[0:128, 0:512],
                                 lnq[kc][:, ts(tt)], wst[kc][:, 0:512],
                                 start=(kc == 0), stop=(kc == NCH - 1))
                        P.wait("vector", "Tn")
                        P.wait("vector", "dG")
                        P.op("vector", "tensor_tensor",
                             vwrite.ap()[:, :].rearrange(
                                 "p (h j) -> p h j", h=NH, j=65)[:, og * 8:(og + 1) * 8, 0:64],
                             pC[0:128, 0:512].rearrange("p (h j) -> p h j", h=8, j=64),
                             r_small["bc_bv"].ap()[:, og * 512:(og + 1) * 512].rearrange(
                                 "p (h j) -> p h j", h=8, j=64),
                             ALU.add, inc="V")
                    P.wait("gpsimd", "V")
                    P.dma("gpsimd", d_V[ts(tile0 + tt), :], vwrite[:, :])
            P.wait("sync", "dG")
            if "kf" in tap_set:
                P.dma("sync", d_taps["kf"][:, :], d_K[:, :])
            if "vt" in tap_set:
                P.dma("sync", d_taps["vt"][:, :], d_V[:, :])
            if "qf" in tap_set:
                P.wait("gpsimd", "V")
                for c_ in range(NCH):
                    P.dma("gpsimd", d_taps["qf"][ts(c_), :], myQ[c_][:, :])
                P.wait("sync", "dG")
            for g in range(8):
                for hp in range(4):
                    nk = 18 + 2 * g
                    for i in range(nk):
                        ib = i % 2
                        P.wait("sync", "Tn")
                        for c_ in range(NCH):
                            P.dma("sync", kb[ib][c_][:, :],
                                  d_K[ts(c_), i * 128:(i + 1) * 128])
                        P.dma("sync", vsb[ib][:, 0:260],
                              d_V[ts(i), hp * 260:(hp + 1) * 260])
                        P.wait("tensor", "dS")
                        for h8 in range(4):
                            h = hp * 4 + h8
                            ch, hh = h // 2, (h % 2) * 64
                            P.op("tensor", "matmul", pE[0:128, 0:256],
                                 kb[ib][ch][hh:hh + 64, :],
                                 myQ[ch][hh:hh + 64, g * 256:(g + 1) * 256],
                                 start=True, stop=True)
                            P.wait("scalar", "Tn")
                            pb = pbuf[h8 % 4]
                            P.wait("scalar", "V", P.now("V"))
                            P.op("scalar", "activation", pb[:, :],
                                 pE[0:128, 0:256], AF.Exp, scale=0.125,
                                 bias=r_small["kbias"][:, g * 34 + i:g * 34 + i + 1],
                                 inc="A")
                            rel = i - 2 * g
                            kind = {0: 0, 1: 1, 16: 2, 17: 3}.get(rel)
                            P.wait("vector", "A")
                            if kind is not None:
                                P.op("vector", "tensor_mul", pb[:, :], pb[:, :],
                                     r_small["trimask"][:, kind * 256:(kind + 1) * 256],
                                     inc="V")
                                P.wait("tensor", "V")
                            else:
                                P.wait("tensor", "A")
                            ps_y = (pA, pB, pC, pD)[h8]
                            yslice = ps_y[0:65, 0:256]
                            P.op("tensor", "matmul", yslice,
                                 vsb[ib][:, h8 * 65:(h8 + 1) * 65], pb[:, :],
                                 start=(i == 0), stop=(i == nk - 1),
                                 skip_group_check=True)
                    for h8 in range(4):
                        h = hp * 4 + h8
                        ch, hh = h // 2, (h % 2) * 64
                        ps_y = (pA, pB, pC, pD)[h8]
                        ysl = ps_y[0:65, 0:256]
                        P.wait("vector", "Tn")
                        P.op("vector", "tensor_copy", yscr[0:65, 0:256], ysl,
                             inc="V")
                        P.wait("vector", "V")
                        P.op("vector", "reciprocal", lrec[0:1, 0:256],
                             yscr[64:65, 0:256], inc="V")
                        P.wait("tensor", "V")
                        P.op("tensor", "matmul", pF[0:64, 0:256],
                             ones1[0:1, :], lrec[0:1, 0:256],
                             start=True, stop=True)
                        P.wait("vector", "Tn")
                        P.op("vector", "tensor_mul",
                             Yfm[ch][hh:hh + 64, g * 256:(g + 1) * 256],
                             yscr[0:64, 0:256], pF[0:64, 0:256], inc="V")
            if "yb" in tap_set:
                P.wait("gpsimd", "V")
                for c_ in range(NCH):
                    P.dma("gpsimd", d_taps["yb"][ts(c_), :], Yfm[c_][:, :])
                P.wait("sync", "dG")
            wproj_sb = sb("b_wproj", (128, NCH * C), BF16)
            P.wait("sync", "Tn")
            P.wait("sync", "dG")
            for kc in range(NCH):
                P.dma("sync", wproj_sb[:, kc * C:(kc + 1) * C],
                      dw["Wproj"][ts(kc), :])
            for tt in range(TH // 128):
                P.wait("sync", "V")
                P.wait("sync", "dG")
                P.dma("sync", xt[0][:, :], d_x2[ts(tt), :])
                P.dma("sync", xt[1][:, :], d_x2[ts(16 + tt), :])
                P.wait("vector", "dS")
                P.wait("vector", "dG")
                P.op("vector", "tensor_scalar_mul", x2my[:, :], xt[0][:, :],
                     r_small["half0"][:, 0:1])
                P.op("vector", "tensor_scalar_mul", x2sc[:, :], xt[1][:, :],
                     r_small["half1"][:, 0:1], inc="V")
                P.wait("vector", "V")
                P.op("vector", "tensor_add", x2my[:, :], x2my[:, :],
                     x2sc[:, :], inc="V")
                for og in range(2):
                    P.wait("tensor", "dS")
                    if True:
                        P.wait("tensor", "V")
                    for kc in range(NCH):
                        P.op("tensor", "matmul", pE[0:128, 0:512],
                             Yfm[kc][:, ts(tt)],
                             wproj_sb[:, kc * C + og * 512: kc * C + (og + 1) * 512],
                             start=(kc == 0), stop=(kc == NCH - 1))
                    P.wait("vector", "Tn")
                    P.op("vector", "tensor_add", x2my[:, og * 512:(og + 1) * 512],
                         x2my[:, og * 512:(og + 1) * 512], pE[0:128, 0:512],
                         inc="V")
                    P.wait("vector", "V")
                    P.op("vector", "tensor_add", x2my[:, og * 512:(og + 1) * 512],
                         x2my[:, og * 512:(og + 1) * 512],
                         r_small["bc_bproj"][:, og * 512:(og + 1) * 512],
                         inc="V")
                P.wait("gpsimd", "V")
                P.dma("gpsimd", d_x3my[ts(tt), :], x2my[:, :])
            P.wait("sync", "dG")

        def phase_barrier():
            for eng in ENGINES:
                for s in ("dS", "dG", "V", "A", "Tn"):
                    P.wait(eng, s)

        load_small_consts()
        prep_weights16(["Wr", "Wk", "Wv", "Wo"])
        if "a" in phases:
            with contextlib.ExitStack() as es_a:
                emit_attn_a(es_a)
            phase_barrier()
        es_w16.close()
        prep_weights16(["w1T", "w2T", "h1T", "h2T", "w3T", "Wattn", "Wproj"])
        phase_barrier()
        if "f1" in phases:
            with contextlib.ExitStack() as es_f:
                emit_ffn(es_f, d_x1, d_x2, T)
            phase_barrier()
        if "x2" in tap_set:
            P.dma("sync", d_taps["x2"][:, :], d_x2[:, :])
        if "b" in phases:
            with contextlib.ExitStack() as es_b:
                emit_attn_b(es_b)
            phase_barrier()
        if "x3my" in tap_set:
            P.dma("sync", d_taps["x3my"][:, :], d_x3my[:, :])
        if "f2" in phases:
            with contextlib.ExitStack() as es_f:
                emit_ffn(es_f, d_x3my, out, TH, tag="f2")
            phase_barrier()
        if "x1" in tap_set:
            P.wait("sync", "dG")
            P.dma("sync", d_taps["x1"][:, :], d_x1[:, :])
        if "f2" not in phases:
            P.wait("sync", "dG")
            P.dma("sync", out[:, :], d_x1[0:TH, :])
        phase_barrier()
        P.wait("gpsimd", "dS")
        P.wait("gpsimd", "dG")

        block = es.enter_context(nc.Block())
        for eng in ENGINES:
            fns = P.fns[eng]
            getattr(block, eng)(lambda e, fns=fns: [f(e) for f in fns])

    return nc, di, out, d_taps


def kernel(**inputs):
    sh, cores = prep_host(inputs)
    nc, di, out, d_taps = build_program()
    in_maps = []
    for core in range(8):
        m = dict(sh)
        m.update(cores[core])
        in_maps.append(m)
    res = run_bass_kernel_spmd(nc, in_maps, core_ids=list(range(8)))
    full = np.empty((4, T, C), np.float32)
    for core in range(8):
        b, par = core // 2, core % 2
        full[b, par * TH:(par + 1) * TH] = res.results[core]["out"]
    return full
